# version 02
# baseline (speedup 1.0000x reference)
import numpy as np
import ml_dtypes
import concourse.bass as bass
import concourse.mybir as mybir
from concourse.bass_utils import run_bass_kernel_spmd

F32 = mybir.dt.float32
BF16 = mybir.dt.bfloat16
AF = mybir.ActivationFunctionType
ALU = mybir.AluOpType
AX = mybir.AxisListType

D = 2048
T = 2048
NC = 16
NH = 16
DH = 128
BLK = 256
NB = T // BLK
TQ = 512
NTQ = T // TQ
EPS_LN = 1e-5
EPS_RMS = 1e-6
DT_MIN = 1e-4
FK0 = 11
T2 = 1024
ENGS = ("sync", "tensor", "vector", "scalar", "gpsimd")
NSLOT = 12
BIG = 1e30
MOE0_SERIALIZE = True


class SemC:
    def __init__(self, h):
        self.h = h
        self.n = 0


CVALS = {EPS_LN: 0, EPS_RMS: 1, 1.0: 2, -0.5: 3, 256.0: 4, -1.0: 5, 0.5: 6, DT_MIN: 7, 0.25: 8}


class Ctx:
    def __init__(self, nc, sems, dram, sb, ps):
        self.nc = nc
        self.q = {e: [] for e in ENGS}
        self.s = sems
        self.dram = dram
        self.sb = sb
        self.ps = ps
        self.nbar = 0
        self.psread = {}
        self.dring = {"S": [SemC(h) for h in sems["ringS"]], "G": [SemC(h) for h in sems["ringG"]]}
        self.didx = {"S": 0, "G": 0}

    def use_ps(self, name):
        if name in self.psread:
            s, v = self.psread.pop(name)
            self.wait("tensor", s, v)

    def ps_consumed(self, name, sem, val):
        self.psread[name] = (sem, val)

    def cv(self, val, nparts=128):
        return self.sb["cvals"][0:nparts, CVALS[val]:CVALS[val] + 1]

    def flush(self, eng):
        sb, ps = self.sb, self.ps
        if eng == "tensor":
            return self.op("tensor", lambda e: e.matmul(
                ps["J"][0:1, 0:8], sb["ones1"][0:1, 0:1], sb["cvals"][0:1, 0:8],
                start=True, stop=True), inc="sT")
        if eng == "vector":
            return self.op("vector", lambda e: e.tensor_copy(sb["junk"][0:1, 0:8], sb["cvals"][0:1, 0:8]), inc="sV")
        if eng == "scalar":
            return self.op("scalar", lambda e: e.activation(sb["junk"][0:1, 8:16], sb["cvals"][0:1, 0:8], AF.Copy), inc="sA")
        if eng == "gpsimd":
            return self.op("gpsimd", lambda e: e.tensor_copy(sb["junk"][0:1, 16:24], sb["cvals"][0:1, 0:8]), inc="sP")
        raise ValueError(eng)

    def dma(self, qeng, out, in_):
        ring = "G" if qeng == "gpsimd" else "S"
        i = self.didx[ring] % NSLOT
        self.didx[ring] += 1
        sem = self.dring[ring][i]
        if sem.n > 0:
            self.q[qeng].append(lambda e, s=sem.h, v=sem.n: e.wait_ge(s, v))
        sem.n += 16
        self.q[qeng].append(lambda e, o=out, i_=in_, s=sem.h: e.dma_start(out=o, in_=i_).then_inc(s, 16))
        return (ring, i, sem.n)

    def wdma(self, eng, tok):
        if tok is None:
            return
        toks = tok if isinstance(tok, list) else [tok]
        best = {}
        for t in toks:
            if t is None:
                continue
            r, i, v = t
            best[(r, i)] = max(best.get((r, i), 0), v)
        for (r, i), v in sorted(best.items()):
            self.q[eng].append(lambda e, s=self.dring[r][i].h, v=v: e.wait_ge(s, v))

    def op(self, eng, fn, inc=None):
        if inc is None:
            self.q[eng].append(fn)
            return None
        sem = self.s[inc]
        sem.n += 1
        self.q[eng].append(lambda e, f=fn, s=sem.h: f(e).then_inc(s, 1))
        return sem.n

    def wait(self, eng, sem, val):
        if val is None or val == 0:
            return
        self.q[eng].append(lambda e, s=self.s[sem].h, v=val: e.wait_ge(s, v))

    def softbar(self):
        for eng in ("tensor", "vector", "scalar", "gpsimd", "sync"):
            for s in ("sT", "sV", "sA", "sP"):
                self.wait(eng, s, self.s[s].n)
        self.wdma("sync", [("S", i, s.n) for i, s in enumerate(self.dring["S"]) if s.n])
        self.wdma("gpsimd", [("G", i, s.n) for i, s in enumerate(self.dring["G"]) if s.n])

    def barrier(self):
        self.nbar += 1
        v = 5 * self.nbar
        self.wdma("sync", [("S", i, s.n) for i, s in enumerate(self.dring["S"]) if s.n])
        self.wdma("gpsimd", [("G", i, s.n) for i, s in enumerate(self.dring["G"]) if s.n])
        for eng in ENGS:
            self.q[eng].append(lambda e, h=self.s["bar"].h: e.sem_inc(h, 1))
        for eng in ENGS:
            self.wait(eng, "bar", v)


def sl(buf, i, w):
    return buf[:, i * w:(i + 1) * w]


def wcol_load(ctx, dst, wname, col0, ncol, nch=NC, row0=0):
    src = ctx.dram[wname].ap().rearrange("(ch p) f -> p ch f", p=128)
    src = src[:, row0:row0 + nch, col0:col0 + ncol]
    dstv = dst[:, 0:nch * ncol].rearrange("p (ch f) -> p ch f", f=ncol)
    return ctx.dma("gpsimd", dstv, src)



def p_layernorm(ctx, src_name, dst_bf, dst_f32=None, stats_name=None):
    sb, ps = ctx.sb, ctx.ps
    src = ctx.dram[src_name].ap()
    dbf = ctx.dram[dst_bf].ap()
    df = ctx.dram[dst_f32].ap() if dst_f32 else None
    dst_st = ctx.dram[stats_name].ap() if stats_name else None
    xs = sb["slabB"]
    x2 = sb["slabC"]
    oaf = sb["slabA"]
    obf = sb["slabA"][:, 8192:12288].bitcast(BF16)

    for tq in range(NTQ):
        t0 = tq * TQ
        toks = [ctx.dma("sync", sl(xs, c, TQ), src[c * 128:(c + 1) * 128, t0:t0 + TQ])
                for c in range(NC)]
        ctx.wdma("scalar", toks)
        for c in range(NC):
            ctx.op("scalar", lambda e, c=c: e.activation(sl(x2, c, TQ), sl(xs, c, TQ), AF.Square),
                   inc="sA" if c == NC - 1 else None)
        vA = ctx.s["sA"].n
        ctx.wdma("tensor", toks)
        ctx.use_ps("A")
        for c in range(NC):
            ctx.op("tensor", lambda e, c=c: e.matmul(
                ps["A"][0:1, :], sb["ones128"][:, 0:1], sl(xs, c, TQ),
                start=(c == 0), stop=(c == NC - 1)))
        ctx.wait("tensor", "sA", vA)
        ctx.use_ps("B")
        for c in range(NC):
            ctx.op("tensor", lambda e, c=c: e.matmul(
                ps["B"][0:1, :], sb["ones128"][:, 0:1], sl(x2, c, TQ),
                start=(c == 0), stop=(c == NC - 1)), inc="sT" if c == NC - 1 else None)
        vT = ctx.s["sT"].n
        ctx.wait("vector", "sT", vT)
        ctx.op("vector", lambda e: e.tensor_scalar(sb["m_row"][0:1, :], ps["A"][0:1, :], 1.0 / D, None, ALU.mult))
        v1 = ctx.op("vector", lambda e: e.tensor_scalar(sb["v_row"][0:1, :], ps["B"][0:1, :], 1.0 / D, None, ALU.mult), inc="sV")
        ctx.ps_consumed("A", "sV", v1)
        ctx.ps_consumed("B", "sV", v1)
        ctx.wait("scalar", "sV", v1)
        v2 = ctx.op("scalar", lambda e: e.activation(sb["m2_row"][0:1, :], sb["m_row"][0:1, :], AF.Square), inc="sA")
        ctx.wait("vector", "sA", v2)
        v3 = ctx.op("vector", lambda e: e.tensor_sub(sb["v_row"][0:1, :], sb["v_row"][0:1, :], sb["m2_row"][0:1, :]), inc="sV")
        ctx.wait("scalar", "sV", v3)
        v4 = ctx.op("scalar", lambda e: e.activation(sb["v_row"][0:1, :], sb["v_row"][0:1, :], AF.Ln, bias=ctx.cv(EPS_LN, 1)), inc="sA")
        ctx.wait("scalar", "sA", v4)
        v5 = ctx.op("scalar", lambda e: e.activation(sb["r_row"][0:1, :], sb["v_row"][0:1, :], AF.Exp, scale=ctx.cv(-0.5, 1)), inc="sA")
        stok = []
        if dst_st is not None:
            ctx.wait("sync", "sA", v5)
            ctx.wait("sync", "sV", v3)
            stok.append(ctx.dma("sync", dst_st[0:1, t0:t0 + TQ], sb["m_row"][0:1, :]))
            stok.append(ctx.dma("sync", dst_st[1:2, t0:t0 + TQ], sb["r_row"][0:1, :]))
        ctx.wait("tensor", "sA", v5)
        ctx.use_ps("C")
        ctx.use_ps("D")
        ctx.op("tensor", lambda e: e.matmul(ps["C"][:, :], sb["ones1"][0:1, :], sb["m_row"][0:1, :], start=True, stop=True))
        vb = ctx.op("tensor", lambda e: e.matmul(ps["D"][:, :], sb["ones1"][0:1, :], sb["r_row"][0:1, :], start=True, stop=True), inc="sT")
        ctx.wait("vector", "sT", vb)
        for c in range(NC):
            ctx.op("vector", lambda e, c=c: e.tensor_sub(sb["tf0"][:, :], sl(xs, c, TQ), ps["C"][:, :]))
            ctx.op("vector", lambda e, c=c: e.tensor_tensor(sl(oaf, c, TQ), sb["tf0"][:, :], ps["D"][:, :], ALU.mult),
                   inc="sV" if c == NC - 1 else None)
        vV = ctx.s["sV"].n
        ctx.ps_consumed("C", "sV", vV)
        ctx.ps_consumed("D", "sV", vV)
        ctx.wait("gpsimd", "sV", vV)
        for c in range(NC):
            ctx.op("gpsimd", lambda e, c=c: e.tensor_copy(sl(obf, c, TQ), sl(oaf, c, TQ)),
                   inc="sP" if c == NC - 1 else None)
        vP = ctx.s["sP"].n
        if df is not None:
            ctx.wait("sync", "sV", vV)
            for c in range(NC):
                stok.append(ctx.dma("sync", df[c * 128:(c + 1) * 128, t0:t0 + TQ], sl(oaf, c, TQ)))
        ctx.wait("sync", "sP", vP)
        for c in range(NC):
            stok.append(ctx.dma("sync", dbf[c * 128:(c + 1) * 128, t0:t0 + TQ], sl(obf, c, TQ)))
        ctx.wdma("vector", stok)
        ctx.wdma("gpsimd", stok)
        ctx.wdma("scalar", stok)
    ctx.barrier()



def load_xn(ctx, src_name):
    sb = ctx.sb
    src = ctx.dram[src_name].ap()
    toks = []
    for c in range(NC):
        base = sb["slabB"].bitcast(BF16) if c < 8 else sb["slabC"].bitcast(BF16)
        toks.append(ctx.dma("sync", sl(base, c % 8, T), src[c * 128:(c + 1) * 128, 0:T]))
    return toks


def xn_tile(ctx, c, t0, w):
    base = ctx.sb["slabB"].bitcast(BF16) if c < 8 else ctx.sb["slabC"].bitcast(BF16)
    return sl(base, c % 8, T)[:, t0:t0 + w]


def p_qkv(ctx):
    sb, ps = ctx.sb, ctx.ps
    ldxn = load_xn(ctx, "XN0B")
    ctx.wdma("tensor", ldxn)
    qt = ctx.dram["QT"].ap()
    kt = ctx.dram["KT"].ap()
    ktok = ctx.dram["KTOK"].ap()
    vtok = ctx.dram["VTOK"].ap()
    wb = [sl(sb["wbuf"], 0, 2048), sl(sb["wbuf"], 1, 2048)]
    wb_free = [None, None]
    so = [sb["tb0"], sb["tb1"]]
    so_freeD = [None, None]
    so_freeT = [None, None]
    tt = [sb["tb2"], sb["tb3"]]
    tt_freeD = [None, None]
    ptn = ["T1", "T2"]
    pt_free = [None, None]
    for fo in range(48):
        b = fo % 2
        ctx.wait("gpsimd", "sT", wb_free[b])
        ldw = wcol_load(ctx, wb[b], "WQKVT", fo * 128, 128)
        ctx.wdma("tensor", ldw)
        for tq in range(NTQ):
            pn = "A" if tq % 2 == 0 else "B"
            pb = ps[pn]
            ctx.use_ps(pn)
            for c in range(NC):
                ctx.op("tensor", lambda e, c=c, pb=pb, b=b, tq=tq: e.matmul(
                    pb[:, :], sl(wb[b], c, 128), xn_tile(ctx, c, tq * TQ, TQ),
                    start=(c == 0), stop=(c == NC - 1)),
                    inc="sT" if c == NC - 1 else None)
            vT = ctx.s["sT"].n
            if tq == NTQ - 1:
                wb_free[b] = vT
            i = (fo * NTQ + tq) % 2
            ctx.wait("scalar", "sT", vT)
            ctx.wdma("scalar", so_freeD[i])
            ctx.wait("scalar", "sT", so_freeT[i])
            vA = ctx.op("scalar", lambda e, pb=pb, i=i: e.activation(so[i][:, :], pb[:, :], AF.Silu), inc="sA")
            ctx.ps_consumed(pn, "sA", vA)
            ctx.wait("sync", "sA", vA)
            t0 = tq * TQ
            if fo < 16:
                h = fo
                so_freeD[i] = ctx.dma("sync", qt[h * 128:(h + 1) * 128, t0:t0 + TQ], so[i][:, :])
                so_freeT[i] = None
            else:
                is_k = fo < 32
                h = fo - 16 if is_k else fo - 32
                so_freeD[i] = (ctx.dma("sync", kt[h * 128:(h + 1) * 128, t0:t0 + TQ], so[i][:, :])
                               if is_k else None)
                j = i
                ctx.wait("tensor", "sA", vA)
                ctx.wait("tensor", "sV", pt_free[j])
                pt = ps[ptn[j]]
                for jj in range(4):
                    ctx.op("tensor", lambda e, jj=jj, i=i, pt=pt: e.transpose(
                        pt[:, jj * 128:(jj + 1) * 128], so[i][:, jj * 128:(jj + 1) * 128], sb["id_bf"][:, :]),
                        inc="sT" if jj == 3 else None)
                so_freeT[i] = ctx.s["sT"].n
                ctx.wait("vector", "sT", ctx.s["sT"].n)
                ctx.wdma("vector", tt_freeD[j])
                vV = ctx.op("vector", lambda e, j=j, pt=pt: e.tensor_copy(tt[j][:, :], pt[:, 0:TQ]), inc="sV")
                pt_free[j] = vV
                ctx.wait("sync", "sV", vV)
                dst = ktok if is_k else vtok
                r0 = h * 2048 + t0
                tt_freeD[j] = ctx.dma("sync",
                        dst[r0:r0 + TQ, 0:128].rearrange("(j p) f -> p j f", p=128),
                        tt[j][:, :].rearrange("p (j f) -> p j f", f=128))
    ctx.barrier()



def p_attention(ctx):
    sb, ps = ctx.sb, ctx.ps
    qt = ctx.dram["QT"].ap()
    kt = ctx.dram["KT"].ap()
    ktok = ctx.dram["KTOK"].ap()
    vtok = ctx.dram["VTOK"].ap()
    att = ctx.dram["ATT"].ap()
    ld0 = [ctx.dma("sync", sb["idx_sb"][:, :].rearrange("p (kc m) -> p kc m", m=256),
                   ctx.dram["IDXT"].ap().rearrange("(kc p) m -> p kc m", p=128)),
           ctx.dma("sync", sb["ar_sb"][:, :], ctx.dram["ARPOS"].ap()[:, :]),
           ctx.dma("sync", sb["arn_sb"][:, :], ctx.dram["ARNEG"].ap()[:, :]),
           ctx.dma("sync", sb["m_row"][0:1, 0:16], ctx.dram["SLOPE"].ap()[0:1, :])]
    ctx.wdma("tensor", ld0)
    ctx.wdma("scalar", ld0)
    ctx.op("tensor", lambda e: e.matmul(ps["A"][:, 0:16], sb["ones1"][0:1, :], sb["m_row"][0:1, 0:16],
                                        start=True, stop=True), inc="sT")
    ctx.wait("vector", "sT", ctx.s["sT"].n)
    vns = ctx.op("vector", lambda e: e.tensor_scalar(sb["negs"][:, :], ps["A"][:, 0:16], -1.0, None, ALU.mult), inc="sV")

    for h in range(NH):
        ctx.wait("scalar", "sV", max(vns, ctx.s["sV"].n))
        ctx.wait("scalar", "sT", ctx.s["sT"].n)
        for kc in range(2):
            ctx.op("scalar", lambda e, kc=kc, h=h: e.activation(
                sl(sb["dd"], kc, 256), sl(sb["idx_sb"], kc, 256), AF.Exp, scale=sb["negs"][:, h:h + 1]))
        for mc in range(2):
            ctx.op("scalar", lambda e, mc=mc, h=h: e.activation(
                sb["qdec"][:, mc:mc + 1], sb["ar_sb"][:, mc:mc + 1], AF.Exp, scale=sb["negs"][:, h:h + 1]))
            ctx.op("scalar", lambda e, mc=mc, h=h: e.activation(
                sb["kdec"][:, mc:mc + 1], sb["arn_sb"][:, mc:mc + 1], AF.Exp, scale=sb["negs"][:, h:h + 1]))
        vA = ctx.op("scalar", lambda e, h=h: e.activation(
            sb["bdec"][:, 0:1], sb["negs"][:, h:h + 1], AF.Exp, scale=ctx.cv(256.0)), inc="sA")
        ctx.op("vector", lambda e: e.memset(sb["kv_f"][:, :], 0.0))
        vkb = ctx.op("vector", lambda e: e.memset(sb["kv_bf"][:, :], 0.0), inc="sV")
        prev_store = None
        v_lastmm = None
        vev = None
        for b in range(NB):
            t0 = b * BLK
            ctx.wdma("sync", prev_store)
            ctx.wait("sync", "sT", v_lastmm)
            r0 = h * 2048 + t0
            ld = [ctx.dma("sync", sb["qt_t"][:, :], qt[h * 128:(h + 1) * 128, t0:t0 + BLK]),
                  ctx.dma("sync", sb["kt_t"][:, :], kt[h * 128:(h + 1) * 128, t0:t0 + BLK]),
                  ctx.dma("sync", sb["ktok_t"][:, :].rearrange("p (j f) -> p j f", f=128),
                          ktok[r0:r0 + BLK, 0:128].rearrange("(j p) f -> p j f", p=128)),
                  ctx.dma("sync", sb["vtok_t"][:, :].rearrange("p (j f) -> p j f", f=128),
                          vtok[r0:r0 + BLK, 0:128].rearrange("(j p) f -> p j f", p=128))]
            ctx.wdma("tensor", ld)
            if b == 0:
                ctx.wait("tensor", "sA", vA)
                ctx.wait("tensor", "sV", vkb)
            ctx.wait("tensor", "sV", vev)
            for kc in range(2):
                ctx.op("tensor", lambda e, kc=kc: e.matmul(
                    ps["A"][:, kc * 256:kc * 256 + 256], sb["kt_t"][:, kc * 128:kc * 128 + 128],
                    sb["qt_t"][:, :], start=True, stop=True),
                    inc="sT" if kc == 1 else None)
            vqk = ctx.s["sT"].n
            ctx.wdma("vector", ld)
            ctx.wait("vector", "sT", vqk)
            ctx.wait("vector", "sA", vA)
            for kc in range(2):
                ctx.op("vector", lambda e, kc=kc: e.tensor_tensor(
                    sl(sb["qkm"], kc, 256), ps["A"][:, kc * 256:kc * 256 + 256], sl(sb["dd"], kc, 256), ALU.mult))
            for kc in range(2):
                ctx.op("vector", lambda e, kc=kc: e.tensor_scalar(
                    sl(sb["kts"], kc, 128), sb["ktok_t"][:, kc * 128:kc * 128 + 128],
                    sb["kdec"][:, kc:kc + 1], None, ALU.mult), inc="sV" if kc == 1 else None)
            vqkm = ctx.s["sV"].n
            ctx.wait("tensor", "sV", vqkm)
            for mc in range(2):
                ctx.op("tensor", lambda e, mc=mc: e.matmul(
                    ps["B"][:, mc * 128:mc * 128 + 128], sb["qt_t"][:, mc * 128:mc * 128 + 128],
                    sb["kv_bf"][:, :], start=True, stop=True))
                for kc in range(2):
                    ctx.op("tensor", lambda e, mc=mc, kc=kc: e.matmul(
                        ps["C"][:, mc * 128:mc * 128 + 128],
                        sl(sb["qkm"], kc, 256)[:, mc * 128:mc * 128 + 128],
                        sb["vtok_t"][:, kc * 128:kc * 128 + 128],
                        start=(kc == 0), stop=(kc == 1)))
            for kc in range(2):
                ctx.op("tensor", lambda e, kc=kc: e.matmul(
                    ps["D"][:, 0:128], sl(sb["kts"], kc, 128), sb["vtok_t"][:, kc * 128:kc * 128 + 128],
                    start=(kc == 0), stop=(kc == 1)), inc="sT" if kc == 1 else None)
            vmm = ctx.s["sT"].n
            v_lastmm = vmm
            ctx.wait("vector", "sT", vmm)
            ctx.wdma("vector", prev_store)
            for mc in range(2):
                ctx.op("vector", lambda e, mc=mc: e.tensor_scalar(
                    sb["tf0"][:, 0:128], ps["B"][:, mc * 128:mc * 128 + 128],
                    sb["qdec"][:, mc:mc + 1], None, ALU.mult))
                ctx.op("vector", lambda e, mc=mc: e.tensor_tensor(
                    sl(sb["attbf"], mc, 128), sb["tf0"][:, 0:128],
                    ps["C"][:, mc * 128:mc * 128 + 128], ALU.add))
            ctx.op("vector", lambda e: e.scalar_tensor_tensor(
                sb["kv_f"][:, :], sb["kv_f"][:, :], sb["bdec"][:, 0:1], ps["D"][:, 0:128], ALU.mult, ALU.add))
            vev = ctx.op("vector", lambda e: e.tensor_copy(sb["kv_bf"][:, :], sb["kv_f"][:, :]), inc="sV")
            vkb = vev
            ctx.wait("tensor", "sV", vev)
            for mc in range(2):
                ctx.op("tensor", lambda e, mc=mc: e.transpose(
                    ps["T1"][:, mc * 128:mc * 128 + 128], sl(sb["attbf"], mc, 128), sb["id_bf"][:, :]),
                    inc="sT" if mc == 1 else None)
            vtr = ctx.s["sT"].n
            ctx.wait("vector", "sT", vtr)
            vcp = ctx.op("vector", lambda e: e.tensor_copy(sb["attfm"][:, :], ps["T1"][:, 0:256]), inc="sV")
            ctx.wait("sync", "sV", vcp)
            prev_store = ctx.dma("sync", att[h * 128:(h + 1) * 128, t0:t0 + BLK], sb["attfm"][:, :])
    ctx.barrier()



def p_gate_out(ctx):
    sb, ps = ctx.sb, ctx.ps
    att = ctx.dram["ATT"].ap()
    sfin = ctx.dram["SFIN"].ap()
    ldn = ctx.dma("sync", sb["normw"][:, :], ctx.dram["NORMW"].ap()[:, :])
    ctx.wdma("vector", ldn)
    zpan = sb["slabA"].bitcast(BF16)
    at = [sb["tb0"], sb["tb1"]]
    at_free = [None, None]
    vrc = None
    atts = sb["slabB"].bitcast(BF16)
    x2 = sb["slabA"]
    vsq = None
    for tq in range(NTQ):
        t0 = tq * TQ
        if vsq is not None:
            ctx.wait("sync", "sA", vsq)
        toks = [ctx.dma("sync", sl(atts, c, TQ), att[c * 128:(c + 1) * 128, t0:t0 + TQ])
                for c in range(NC)]
        ctx.wdma("scalar", toks)
        if vrc is not None:
            ctx.wait("scalar", "sV", vrc)
        for c in range(NC):
            ctx.op("scalar", lambda e, c=c: e.activation(sl(x2, c, TQ), sl(atts, c, TQ), AF.Square),
                   inc="sA" if c == NC - 1 else None)
        vsq = ctx.s["sA"].n
        ctx.wait("tensor", "sA", vsq)
        ctx.use_ps("A")
        for c in range(NC):
            ctx.op("tensor", lambda e, c=c: e.matmul(
                ps["A"][0:1, :], sb["ones128"][:, 0:1], sl(x2, c, TQ),
                start=(c == 0), stop=(c == NC - 1)), inc="sT" if c == NC - 1 else None)
        fT = ctx.flush("tensor")
        ctx.wait("vector", "sT", fT)
        v1 = ctx.op("vector", lambda e: e.tensor_scalar(sb["v_row"][0:1, :], ps["A"][0:1, :], 1.0 / D, None, ALU.mult), inc="sV")
        fV = ctx.flush("vector")
        ctx.ps_consumed("A", "sV", fV)
        ctx.wait("scalar", "sV", fV)
        v4 = ctx.op("scalar", lambda e: e.activation(sb["v_row"][0:1, :], sb["v_row"][0:1, :], AF.Ln, bias=ctx.cv(EPS_RMS, 1)), inc="sA")
        ctx.wait("scalar", "sA", v4)
        v5 = ctx.op("scalar", lambda e: e.activation(sb["r_row"][0:1, :], sb["v_row"][0:1, :], AF.Exp, scale=ctx.cv(-0.5, 1)), inc="sA")
        fA = ctx.flush("scalar")
        ctx.wait("tensor", "sA", fA)
        ctx.use_ps("B")
        vb = ctx.op("tensor", lambda e: e.matmul(ps["B"][:, :], sb["ones1"][0:1, :], sb["r_row"][0:1, :],
                                                 start=True, stop=True), inc="sT")
        fT2 = ctx.flush("tensor")
        ctx.wait("vector", "sT", fT2)
        vrc = ctx.op("vector", lambda e, tq=tq: e.tensor_copy(sl(sb["rstdb"], tq, TQ), ps["B"][:, :]), inc="sV")
        vrc = ctx.flush("vector")
        ctx.ps_consumed("B", "sV", vrc)
    vrstd = ctx.s["sV"].n
    ctx.wait("sync", "sA", ctx.s["sA"].n)
    ldxn = load_xn(ctx, "XN0B")
    wb = [sl(sb["wbuf"], 0, 2048), sl(sb["wbuf"], 1, 2048)]
    wb_free = [None, None]
    gt_free = None
    ctx.wdma("tensor", ldxn)
    for fo in range(NC):
        bw = fo % 2
        ctx.wait("gpsimd", "sT", wb_free[bw])
        ldw = wcol_load(ctx, wb[bw], "WGATET", fo * 128, 128)
        ctx.wdma("tensor", ldw)
        for tq in range(NTQ):
            pn = "A" if tq % 2 == 0 else "B"
            pb = ps[pn]
            ctx.use_ps(pn)
            for c in range(NC):
                ctx.op("tensor", lambda e, c=c, bw=bw, tq=tq, pb=pb: e.matmul(
                    pb[:, :], sl(wb[bw], c, 128), xn_tile(ctx, c, tq * TQ, TQ),
                    start=(c == 0), stop=(c == NC - 1)), inc="sT" if c == NC - 1 else None)
            vT = ctx.s["sT"].n
            if tq == NTQ - 1:
                wb_free[bw] = vT
            i = (fo * NTQ + tq) % 2
            ctx.wait("scalar", "sT", vT)
            ctx.wait("scalar", "sV", gt_free)
            vg = ctx.op("scalar", lambda e, pb=pb: e.activation(sb["gtile"][:, :], pb[:, :], AF.Sigmoid), inc="sA")
            ctx.ps_consumed(pn, "sA", vg)
            ctx.wait("sync", "sV", at_free[i])
            ld = ctx.dma("sync", at[i][:, :], att[fo * 128:(fo + 1) * 128, tq * TQ:tq * TQ + TQ])
            ctx.wdma("vector", ld)
            ctx.wait("vector", "sA", vg)
            if fo == 0 and tq == 0:
                ctx.wait("vector", "sV", vrstd)
            ctx.op("vector", lambda e, i=i, tq=tq: e.tensor_tensor(
                sb["tf1"][:, :], at[i][:, :], sl(sb["rstdb"], tq, TQ), ALU.mult))
            ctx.op("vector", lambda e, fo=fo: e.tensor_scalar(
                sb["tf2"][:, :], sb["tf1"][:, :], sb["normw"][:, fo:fo + 1], None, ALU.mult))
            vz = ctx.op("vector", lambda e, fo=fo, tq=tq: e.tensor_tensor(
                zpan[:, fo * 2048 + tq * TQ:fo * 2048 + tq * TQ + TQ],
                sb["tf2"][:, :], sb["gtile"][:, :], ALU.mult), inc="sV")
            at_free[i] = vz
            gt_free = vz
    vzdone = ctx.s["sV"].n
    ctx.wait("sync", "sV", vzdone)
    for fo in range(NC):
        ctx.dma("sync", ctx.dram["ZP"].ap()[fo * 128:(fo + 1) * 128, :], zpan[:, fo * 2048:(fo + 1) * 2048])
    ctx.dma("sync", ctx.dram["RSDB"].ap()[:, :], sb["rstdb"][:, :].rearrange("p (tq w) -> p tq w", w=TQ))
    ctx.wait("tensor", "sV", vzdone)
    sf = [sb["tf1"], sb["tf2"]]
    sf_free = [None, None]
    for co in range(NC):
        bw = co % 2
        ctx.wait("gpsimd", "sT", wb_free[bw])
        ldw = wcol_load(ctx, wb[bw], "WOUTT", co * 128, 128)
        ctx.wdma("tensor", ldw)
        for tq in range(NTQ):
            pn = "A" if tq % 2 == 0 else "B"
            pb = ps[pn]
            ctx.use_ps(pn)
            for fo in range(NC):
                ctx.op("tensor", lambda e, fo=fo, bw=bw, tq=tq, pb=pb: e.matmul(
                    pb[:, :], sl(wb[bw], fo, 128),
                    zpan[:, fo * 2048 + tq * TQ:fo * 2048 + tq * TQ + TQ],
                    start=(fo == 0), stop=(fo == NC - 1)), inc="sT" if fo == NC - 1 else None)
            vT = ctx.s["sT"].n
            if tq == NTQ - 1:
                wb_free[bw] = vT
            i = (co * NTQ + tq) % 2
            ctx.wait("vector", "sT", vT)
            ctx.wdma("vector", sf_free[i])
            vv = ctx.op("vector", lambda e, i=i, pb=pb: e.tensor_copy(sf[i][:, :], pb[:, :]), inc="sV")
            ctx.ps_consumed(pn, "sV", vv)
            ctx.wait("sync", "sV", vv)
            sf_free[i] = ctx.dma("sync", sfin[co * 128:(co + 1) * 128, tq * TQ:tq * TQ + TQ], sf[i][:, :])
    ctx.barrier()





def p_router0(ctx):
    sb, ps = ctx.sb, ctx.ps
    xin = ctx.dram["XIN"].ap()
    xs = sb["slabB"]
    lg = sb["slabC"][0:9, 0:T]
    st_m = sb["slabC"][32:33, 0:T]
    st_r = sb["slabC"][64:65, 0:T]
    rt = sb["rt"]
    tgw = ctx.dma("sync", sb["tf3"][:, 0:144].rearrange("p (ch e) -> p ch e", e=9),
                  ctx.dram["GATEWT"].ap().rearrange("(ch p) e -> p ch e", p=128))
    tst = [ctx.dma("sync", st_m, ctx.dram["STATS0"].ap()[0:1, :]),
           ctx.dma("sync", st_r, ctx.dram["STATS0"].ap()[1:2, :])]
    tid = ctx.dma("sync", sb["id_f"][:, :], ctx.dram["IDN"].ap()[:, :])
    ctx.wdma("tensor", [tgw, tid] + tst)
    ctx.use_ps("D")
    for c in range(NC):
        ctx.op("tensor", lambda e, c=c: e.matmul(
            ps["D"][0:9, 0:1], sb["tf3"][:, c * 9:(c + 1) * 9], sb["ones128"][:, 0:1],
            start=(c == 0), stop=(c == NC - 1)), inc="sT" if c == NC - 1 else None)
    fT = ctx.flush("tensor")
    ctx.wait("vector", "sT", fT)
    vcs = ctx.op("vector", lambda e: e.tensor_copy(rt[0:9, 40:41], ps["D"][0:9, 0:1]), inc="sV")
    vcs = ctx.flush("vector")
    ctx.ps_consumed("D", "sV", vcs)
    ctx.wait("tensor", "sV", vcs)
    ctx.op("tensor", lambda e: e.matmul(ps["D"][0:1, 0:9], rt[0:9, 40:41], sb["id_f"][0:9, 0:9],
                                        start=True, stop=True))
    fT = ctx.flush("tensor")
    ctx.wait("vector", "sT", fT)
    vcs = ctx.op("vector", lambda e: e.tensor_copy(rt[0:1, 41:50], ps["D"][0:1, 0:9]), inc="sV")
    vcs = ctx.flush("vector")
    ctx.ps_consumed("D", "sV", vcs)
    ctx.wait("tensor", "sV", vcs)
    ctx.op("tensor", lambda e: e.matmul(ps["D"][:, 0:9], sb["ones1"][0:1, :], rt[0:1, 41:50],
                                        start=True, stop=True))
    fT = ctx.flush("tensor")
    ctx.wait("vector", "sT", fT)
    vcs = ctx.op("vector", lambda e: e.tensor_copy(rt[:, 50:59], ps["D"][:, 0:9]), inc="sV")
    vcs = ctx.flush("vector")
    ctx.ps_consumed("D", "sV", vcs)
    vlg = None
    fT = None
    for tq in range(NTQ):
        ctx.wait("sync", "sT", fT)
        toks = [ctx.dma("sync", sl(xs, c, TQ), xin[c * 128:(c + 1) * 128, tq * TQ:tq * TQ + TQ])
                for c in range(NC)]
        ctx.wdma("tensor", toks)
        pn = "A" if tq % 2 == 0 else "B"
        ctx.use_ps(pn)
        for c in range(NC):
            ctx.op("tensor", lambda e, c=c, pn=pn: e.matmul(
                ctx.ps[pn][0:9, :], sb["tf3"][:, c * 9:(c + 1) * 9], sl(xs, c, TQ),
                start=(c == 0), stop=(c == NC - 1)), inc="sT" if c == NC - 1 else None)
        fT = ctx.flush("tensor")
        ctx.wait("vector", "sT", fT)
        vlg = ctx.op("vector", lambda e, tq=tq, pn=pn: e.tensor_copy(lg[:, tq * TQ:tq * TQ + TQ], ctx.ps[pn][0:9, :]), inc="sV")
        ctx.ps_consumed(pn, "sV", vlg)
    vlg = ctx.flush("vector")
    ctx.wait("sync", "sV", vlg)
    ctx.dma("sync", ctx.dram["LGD"].ap()[0:9, :], lg)
    ctx.wait("tensor", "sV", vlg)
    vcm = None
    for tt in range(16):
        c0 = tt * 128
        ctx.use_ps("C")
        ctx.op("tensor", lambda e, c0=c0: e.matmul(
            ps["C"][:, 0:9], lg[0:9, c0:c0 + 128], sb["id_f"][0:9, 0:9], start=True, stop=True))
        ctx.use_ps("D")
        ctx.op("tensor", lambda e, c0=c0: e.matmul(
            ps["D"][:, 0:1], st_m[0:1, c0:c0 + 128], sb["id_f"][32:33, 32:33], start=True, stop=True))
        ctx.op("tensor", lambda e, c0=c0: e.matmul(
            ps["D"][:, 1:2], st_r[0:1, c0:c0 + 128], sb["id_f"][64:65, 64:65], start=True, stop=True))
        fT = ctx.flush("tensor")
        ctx.wait("vector", "sT", fT)

        def V(fn):
            v = ctx.op("vector", fn, inc="sV")
            ctx.wait("vector", "sV", v)
            return v

        V(lambda e: e.tensor_copy(rt[:, 0:9], ps["C"][:, 0:9]))
        v0 = V(lambda e: e.tensor_copy(rt[:, 60:62], ps["D"][:, 0:2]))
        ctx.ps_consumed("C", "sV", v0)
        ctx.ps_consumed("D", "sV", v0)
        V(lambda e: e.tensor_scalar(rt[:, 62:63], rt[:, 60:61], -1.0, None, ALU.mult))
        V(lambda e: e.scalar_tensor_tensor(
            rt[:, 10:19], rt[:, 50:59], rt[:, 62:63], rt[:, 0:9], ALU.mult, ALU.add))
        v1b = V(lambda e: e.tensor_scalar(rt[:, 0:9], rt[:, 10:19], rt[:, 61:62], None, ALU.mult))
        V(lambda e: e.reduce_max(rt[:, 44:45], rt[:, 0:8], axis=AX.X))
        v3 = V(lambda e: e.tensor_scalar(rt[:, 45:46], rt[:, 44:45], -1.0, None, ALU.mult))
        ctx.wait("scalar", "sV", v3)
        v4 = ctx.op("scalar", lambda e: e.activation(rt[:, 20:28], rt[:, 0:8], AF.Exp, bias=rt[:, 45:46]), inc="sA")
        fA = ctx.flush("scalar")
        ctx.wait("vector", "sA", fA)
        V(lambda e: e.reduce_sum(rt[:, 46:47], rt[:, 20:28], axis=AX.X))
        V(lambda e: e.reciprocal(rt[:, 47:48], rt[:, 46:47]))
        V(lambda e: e.tensor_scalar(rt[:, 20:28], rt[:, 20:28], rt[:, 47:48], None, ALU.mult))
        V(lambda e: e.reduce_max(rt[:, 48:49], rt[:, 20:28], axis=AX.X))
        V(lambda e: e.tensor_scalar(rt[:, 28:36], rt[:, 20:28], rt[:, 48:49], None, ALU.is_ge))
        V(lambda e: e.scalar_tensor_tensor(
            rt[:, 36:44], rt[:, 28:36], -BIG, rt[:, 20:28], ALU.mult, ALU.add))
        V(lambda e: e.reduce_max(rt[:, 49:50], rt[:, 36:44], axis=AX.X))
        V(lambda e: e.tensor_scalar(rt[:, 36:44], rt[:, 36:44], rt[:, 49:50], None, ALU.is_ge))
        V(lambda e: e.tensor_add(rt[:, 28:36], rt[:, 28:36], rt[:, 36:44]))
        vall = V(lambda e: e.tensor_mul(rt[:, 10:18], rt[:, 20:28], rt[:, 28:36]))
        ctx.wait("scalar", "sV", v1b)
        vsg = ctx.op("scalar", lambda e: e.activation(rt[:, 18:19], rt[:, 8:9], AF.Sigmoid), inc="sA")
        ctx.wait("tensor", "sV", vall)
        ctx.wait("tensor", "sA", vsg)
        ctx.use_ps("C")
        ctx.op("tensor", lambda e: e.matmul(ps["C"][0:9, 0:128], rt[:, 10:19], sb["id_f"][:, 0:128],
                                            start=True, stop=True))
        fT = ctx.flush("tensor")
        ctx.wait("vector", "sT", fT)
        vcm = ctx.op("vector", lambda e, c0=c0: e.tensor_copy(sb["cmb_sb"][0:9, c0:c0 + 128], ps["C"][0:9, 0:128]), inc="sV")
        ctx.ps_consumed("C", "sV", vcm)
        ctx.wait("vector", "sT", fT)
    vcm = ctx.flush("vector")
    ctx.wait("sync", "sV", vcm)
    ctx.dma("sync", ctx.dram["CMB"].ap()[0:9, :], sb["cmb_sb"][0:9, :])
    ctx.barrier()



def p_moe0(ctx):
    sb, ps = ctx.sb, ctx.ps
    xn0 = ctx.dram["XN0B"].ap()
    xin = ctx.dram["XIN"].ap()
    sfin = ctx.dram["SFIN"].ap()
    x1 = ctx.dram["X1"].ap()
    tcm = ctx.dma("sync", sb["cmb_sb"][0:9, :], ctx.dram["CMB"].ap()[0:9, :])
    ctx.wdma("tensor", tcm)
    ybuf = sb["slabA"]
    hpan = sb["slabC"].bitcast(BF16)
    wb = [sl(sb["wbuf"], 0, 2048), sl(sb["wbuf"], 1, 2048),
          sl(sb["wbuf2"], 0, 2048), sl(sb["wbuf2"], 1, 2048)]
    wfreeB = [None, None, None, None]
    for t2 in range(2):
        tc0 = t2 * T2
        xnh = sb["slabB"].bitcast(BF16)
        xtoks = [ctx.dma("sync", sl(xnh, c, T2), xn0[c * 128:(c + 1) * 128, tc0:tc0 + T2])
                 for c in range(NC)]
        ctx.wdma("tensor", xtoks)
        if t2 == 0:
            y_free = [None] * NC
        hp_free = None
        tf1_carry = vo_last if t2 > 0 else None
        for ex in range(12):
            erow = ex if ex < 8 else 8
            for th in range(2):
                ctx.use_ps("A")
                ctx.op("tensor", lambda e, erow=erow, th=th, tc0=tc0: e.matmul(
                    ps["A"][:, :], sb["sel8_sb"][0:9, erow * 128:erow * 128 + 128],
                    sb["cmb_sb"][0:9, tc0 + th * TQ:tc0 + th * TQ + TQ], start=True, stop=True))
                fT = ctx.flush("tensor")
                ctx.wait("vector", "sT", fT)
                vcb = ctx.op("vector", lambda e, th=th: e.tensor_copy(sb["cb"][:, th * TQ:th * TQ + TQ], ps["A"][:, :]), inc="sV")
                ctx.ps_consumed("A", "sV", vcb)
            vcb = ctx.flush("vector")
            if ex < 8:
                w1n, w1r = "FC1T", ex * 16
                w2n, w2r = "FC2T", ex * 16
            else:
                w1n, w1r = "SFC1T", 0
                w2n, w2r = "SFC2T", 0
            tf1_free = tf1_carry
            tf1_carry = None
            for fk in range(FK0):
                col = fk * 128 if ex < 8 else (ex - 8) * 1408 + fk * 128
                b1 = (fk % 2) * 2
                b2 = b1 + 1
                ctx.wait("gpsimd", "sT", wfreeB[b1])
                lw1 = wcol_load(ctx, wb[b1], w1n, col, 128, row0=w1r)
                ctx.wait("gpsimd", "sT", wfreeB[b2])
                lw2 = wcol_load(ctx, wb[b2], w2n, col, 128, row0=w2r)
                ctx.wdma("tensor", [lw1, lw2])
                for th in range(2):
                    ctx.use_ps("A")
                    ctx.use_ps("B")
                    for c in range(NC):
                        ctx.op("tensor", lambda e, c=c, th=th, b1=b1: e.matmul(
                            ps["A"][:, :], sl(wb[b1], c, 128), sl(xnh, c, T2)[:, th * TQ:th * TQ + TQ],
                            start=(c == 0), stop=(c == NC - 1)))
                    for c in range(NC):
                        ctx.op("tensor", lambda e, c=c, th=th, b2=b2: e.matmul(
                            ps["B"][:, :], sl(wb[b2], c, 128), sl(xnh, c, T2)[:, th * TQ:th * TQ + TQ],
                            start=(c == 0), stop=(c == NC - 1)), inc="sT" if c == NC - 1 else None)
                    vT = ctx.s["sT"].n
                    if th == 1:
                        wfreeB[b1] = vT
                        wfreeB[b2] = vT
                    ctx.wait("scalar", "sT", vT)
                    ctx.wait("scalar", "sV", tf1_free)
                    vs1 = ctx.op("scalar", lambda e: e.activation(sb["tf1"][:, :], ps["A"][:, :], AF.Silu), inc="sA")
                    ctx.ps_consumed("A", "sA", vs1)
                    ctx.wait("vector", "sA", vs1)
                    ctx.wait("vector", "sV", vcb)
                    ctx.wait("vector", "sT", hp_free)
                    ctx.op("vector", lambda e, th=th: e.tensor_tensor(
                        sb["tf2"][:, :], ps["B"][:, :], sb["cb"][:, th * TQ:th * TQ + TQ], ALU.mult))
                    vh = ctx.op("vector", lambda e, fk=fk, th=th: e.tensor_tensor(
                        hpan[:, fk * 1024 + th * TQ:fk * 1024 + th * TQ + TQ],
                        sb["tf1"][:, :], sb["tf2"][:, :], ALU.mult), inc="sV")
                    ctx.ps_consumed("B", "sV", vh)
                    tf1_free = vh
            vhall = ctx.s["sV"].n
            ctx.wait("tensor", "sV", vhall)
            if ex < 8:
                pn_, pr = "PROJT", ex * 11
            else:
                pn_, pr = "SPROJT", (ex - 8) * 11
            for co in range(NC):
                bw = co % 4
                ctx.wait("gpsimd", "sT", wfreeB[bw])
                lpw = wcol_load(ctx, wb[bw], pn_, co * 128, 128, nch=FK0, row0=pr)
                ctx.wdma("tensor", lpw)
                for th in range(2):
                    pnm = "C" if th == 0 else "D"
                    ctx.use_ps(pnm)
                    for fk in range(FK0):
                        ctx.op("tensor", lambda e, fk=fk, bw=bw, th=th, pnm=pnm: e.matmul(
                            ps[pnm][:, :], sl(wb[bw], fk, 128),
                            hpan[:, fk * 1024 + th * TQ:fk * 1024 + th * TQ + TQ],
                            start=(fk == 0), stop=(fk == FK0 - 1)), inc="sT" if fk == FK0 - 1 else None)
                    vT = ctx.s["sT"].n
                    if th == 1:
                        wfreeB[bw] = vT
                    hp_free = vT
                    ctx.wait("vector", "sT", vT)
                    ctx.wait("vector", "sA", y_free[co])
                    if ex == 0:
                        vy = ctx.op("vector", lambda e, co=co, th=th, pnm=pnm: e.tensor_copy(
                            sl(ybuf, co, T2)[:, th * TQ:th * TQ + TQ], ps[pnm][:, :]), inc="sV")
                    else:
                        vy = ctx.op("vector", lambda e, co=co, th=th, pnm=pnm: e.tensor_tensor(
                            sl(ybuf, co, T2)[:, th * TQ:th * TQ + TQ],
                            sl(ybuf, co, T2)[:, th * TQ:th * TQ + TQ], ps[pnm][:, :], ALU.add), inc="sV")
                    ctx.ps_consumed(pnm, "sV", vy)
            if MOE0_SERIALIZE:
                ctx.softbar()
        vyall = ctx.s["sV"].n
        ctx.wait("sync", "sV", vyall)
        for co in range(NC):
            ctx.dma("sync", ctx.dram["YMO"].ap()[co * 128:(co + 1) * 128, tc0:tc0 + T2], sl(ybuf, co, T2))
        ctx.wait("scalar", "sV", vyall)
        for co in range(NC):
            for th in range(2):
                vg = ctx.op("scalar", lambda e, co=co, th=th: e.activation(
                    sb["tf1"][:, :], sl(ybuf, co, T2)[:, th * TQ:th * TQ + TQ], AF.Silu), inc="sA")
                tx = ctx.dma("sync", sb["tf2"][:, :], xin[co * 128:(co + 1) * 128, tc0 + th * TQ:tc0 + th * TQ + TQ])
                ts_ = ctx.dma("sync", sb["tf3"][:, :], sfin[co * 128:(co + 1) * 128, tc0 + th * TQ:tc0 + th * TQ + TQ])
                ctx.wdma("vector", [tx, ts_])
                ctx.wait("vector", "sA", vg)
                ctx.op("vector", lambda e: e.tensor_mul(sb["tf0"][:, :], sb["tf3"][:, :], sb["tf1"][:, :]))
                vo = ctx.op("vector", lambda e: e.tensor_add(sb["tf1"][:, :], sb["tf2"][:, :], sb["tf0"][:, :]), inc="sV")
                ctx.wait("sync", "sV", vo)
                tout = ctx.dma("sync", x1[co * 128:(co + 1) * 128, tc0 + th * TQ:tc0 + th * TQ + TQ], sb["tf1"][:, :])
                ctx.wait("scalar", "sV", vo)
                ctx.wdma("scalar", tout)
                ctx.wdma("vector", tout)
                vo_last = vo
        y_free = [ctx.s["sA"].n] * NC
        ctx.wait("sync", "sT", ctx.s["sT"].n)
    ctx.barrier()



def p_dtbc(ctx):
    sb, ps = ctx.sb, ctx.ps
    ldxn = load_xn(ctx, "XN1B")
    ctx.wdma("tensor", ldxn)
    tb_ = [ctx.dma("sync", sb["bia_sb"][:, 0:16], ctx.dram["BDT"].ap()[:, :]),
           ctx.dma("sync", sb["bia_sb"][:, 16:32], ctx.dram["BB"].ap()[:, :]),
           ctx.dma("sync", sb["bia_sb"][:, 32:48], ctx.dram["BC"].ap()[:, :])]
    ctx.wdma("vector", tb_)
    wb = [sl(sb["wbuf"], 0, 2048), sl(sb["wbuf"], 1, 2048)]
    wb_free = [None, None]
    out_free = None
    for wi, (wname, oname, bcol) in enumerate(
            (("WDT", "DTP", 0), ("WB", "BTP", 16), ("WC", "CTP", 32))):
        dst = ctx.dram[oname].ap()
        is_dt = wi == 0
        for eo in range(NC):
            bw = eo % 2
            ctx.wait("gpsimd", "sT", wb_free[bw])
            ldw = wcol_load(ctx, wb[bw], wname, eo * 128, 128)
            ctx.wdma("tensor", ldw)
            for tq in range(NTQ):
                pn = "A" if tq % 2 == 0 else "B"
                ctx.use_ps(pn)
                for c in range(NC):
                    ctx.op("tensor", lambda e, c=c, bw=bw, tq=tq, pn=pn: e.matmul(
                        ps[pn][:, :], sl(wb[bw], c, 128), xn_tile(ctx, c, tq * TQ, TQ),
                        start=(c == 0), stop=(c == NC - 1)), inc="sT" if c == NC - 1 else None)
                vT = ctx.s["sT"].n
                if tq == NTQ - 1:
                    wb_free[bw] = vT
                ctx.wait("vector", "sT", vT)
                ctx.wdma("vector", out_free)
                vz = ctx.op("vector", lambda e, eo=eo, pn=pn, bcol=bcol: e.tensor_scalar(
                    sb["tf1"][:, :], ps[pn][:, :], sb["bia_sb"][:, bcol + eo:bcol + eo + 1], None, ALU.add), inc="sV")
                ctx.ps_consumed(pn, "sV", vz)
                if not is_dt:
                    ctx.wait("sync", "sV", vz)
                    out_free = ctx.dma("sync", dst[eo * 128:(eo + 1) * 128, tq * TQ:tq * TQ + TQ], sb["tf1"][:, :])
                    ctx.wait("vector", "sV", vz)
                else:
                    fV = ctx.flush("vector")
                    ctx.wait("scalar", "sV", fV)
                    ctx.op("scalar", lambda e: e.activation(sb["tf2"][:, :], sb["tf1"][:, :], AF.Abs))
                    ctx.op("scalar", lambda e: e.activation(sb["tf2"][:, :], sb["tf2"][:, :], AF.Exp, scale=ctx.cv(-1.0)))
                    ctx.op("scalar", lambda e: e.activation(sb["tf2"][:, :], sb["tf2"][:, :], AF.Ln, bias=ctx.cv(1.0)))
                    va = ctx.op("scalar", lambda e: e.activation(sb["tf3"][:, :], sb["tf1"][:, :], AF.Relu), inc="sA")
                    ctx.wait("vector", "sA", va)
                    vd = ctx.op("vector", lambda e: e.scalar_tensor_tensor(
                        sb["tf0"][:, :], sb["tf3"][:, :], DT_MIN, sb["tf2"][:, :], ALU.add, ALU.add), inc="sV")
                    ctx.wait("sync", "sV", vd)
                    out_free = ctx.dma("sync", dst[eo * 128:(eo + 1) * 128, tq * TQ:tq * TQ + TQ], sb["tf0"][:, :])
                    ctx.wdma("scalar", out_free)
                    ctx.wait("scalar", "sV", vd)
    ctx.barrier()



def p_scan(ctx):
    sb, ps = ctx.sb, ctx.ps
    A = sb["slabA"]
    dt_t = sl(A, 0, TQ); u_t = sl(A, 1, TQ); b_t = sl(A, 2, TQ); c_t = sl(A, 3, TQ)
    wg_t = sl(A, 4, TQ); zg_t = sl(A, 5, TQ)
    xg = [sl(A, 6, TQ), sl(A, 7, TQ)]
    ab = [sl(A, 8, TQ), sl(A, 9, TQ)]
    bu = [sl(A, 10, TQ), sl(A, 11, TQ)]
    yo_t = sl(A, 12, TQ)
    dtp = ctx.dram["DTP"].ap(); btp = ctx.dram["BTP"].ap(); ctp = ctx.dram["CTP"].ap()
    xn1f = ctx.dram["XN1F"].ap(); ys = ctx.dram["YS"].ap()
    ta = ctx.dma("sync", sb["A_sb"][:, :].rearrange("p (g n) -> p g n", n=16),
                 ctx.dram["ARAW"].ap().rearrange("(g p) n -> p g n", p=128))
    tbb = ctx.dma("sync", sb["B_sb"][:, :].rearrange("p (g n) -> p g n", n=16),
                  ctx.dram["BBASE"].ap().rearrange("(g p) n -> p g n", p=128))
    tcc = ctx.dma("sync", sb["C_sb"][:, :].rearrange("p (g n) -> p g n", n=16),
                  ctx.dram["CBASE"].ap().rearrange("(g p) n -> p g n", p=128))
    td = ctx.dma("sync", sb["D_sb"][:, :], ctx.dram["DVEC"].ap()[:, :])
    ctx.wdma("scalar", [ta, tbb, tcc, td])
    ctx.op("scalar", lambda e: e.activation(sb["tf1"][:, 0:256], sb["A_sb"][:, :], AF.Abs))
    ctx.op("scalar", lambda e: e.activation(sb["tf1"][:, 0:256], sb["tf1"][:, 0:256], AF.Exp, scale=ctx.cv(-1.0)))
    ctx.op("scalar", lambda e: e.activation(sb["tf1"][:, 0:256], sb["tf1"][:, 0:256], AF.Ln, bias=ctx.cv(1.0)))
    va = ctx.op("scalar", lambda e: e.activation(sb["tf2"][:, 0:256], sb["A_sb"][:, :], AF.Relu), inc="sA")
    fA0 = ctx.flush("scalar")
    ctx.wait("vector", "sA", fA0)
    ctx.op("vector", lambda e: e.tensor_add(sb["tf1"][:, 0:256], sb["tf1"][:, 0:256], sb["tf2"][:, 0:256]))
    vA_ = ctx.op("vector", lambda e: e.tensor_scalar(sb["A_sb"][:, :], sb["tf1"][:, 0:256], -1.0, None, ALU.mult), inc="sV")
    ctx.op("vector", lambda e: e.memset(sb["state"][:, :], 0.0))
    fV0 = ctx.flush("vector")
    ctx.wait("scalar", "sV", fV0)
    NCH = 4
    st_tok = None
    scan_done = [None, None]
    last_va2 = None
    vy = None
    for ch in range(NCH):
        tc0 = ch * TQ
        for g in range(16):
            row = g * 128
            ctx.wait("sync", "sA", last_va2 or 0)
            ctx.wait("sync", "sV", vy or 0)
            toks = [ctx.dma("sync", dt_t, dtp[row:row + 128, tc0:tc0 + TQ]),
                    ctx.dma("sync", u_t, xn1f[row:row + 128, tc0:tc0 + TQ]),
                    ctx.dma("sync", b_t, btp[row:row + 128, tc0:tc0 + TQ]),
                    ctx.dma("sync", c_t, ctp[row:row + 128, tc0:tc0 + TQ])]
            ctx.wdma("vector", toks)
            ctx.wdma("scalar", toks)
            ctx.wait("vector", "sA", last_va2)
            ctx.op("vector", lambda e: e.tensor_mul(wg_t, dt_t, b_t))
            vw = ctx.op("vector", lambda e: e.tensor_mul(wg_t, wg_t, u_t), inc="sV")
            ctx.wait("scalar", "sV", vw)
            for n in range(16):
                gn = g * 16 + n
                i = n % 2
                if scan_done[i] is not None:
                    ctx.wait("scalar", "sV", scan_done[i])
                va1 = ctx.op("scalar", lambda e, gn=gn, i=i: e.activation(
                    ab[i], dt_t, AF.Exp, scale=sb["A_sb"][:, gn:gn + 1]))
                va2 = ctx.op("scalar", lambda e, gn=gn, i=i: e.activation(
                    bu[i], wg_t, AF.Copy, scale=sb["B_sb"][:, gn:gn + 1]), inc="sA")
                ctx.wait("vector", "sA", va2)
                vscan = ctx.op("vector", lambda e, gn=gn, i=i: e.tensor_tensor_scan(
                    xg[i], ab[i], bu[i], sb["state"][:, gn:gn + 1], ALU.mult, ALU.add), inc="sV")
                ctx.wait("vector", "sV", vscan)
                ctx.op("vector", lambda e, gn=gn, i=i: e.tensor_copy(sb["state"][:, gn:gn + 1], xg[i][:, TQ - 1:TQ]))
                if n == 0:
                    vz = ctx.op("vector", lambda e, gn=gn, i=i: e.tensor_scalar(
                        zg_t, xg[i], sb["C_sb"][:, gn:gn + 1], None, ALU.mult), inc="sV")
                else:
                    vz = ctx.op("vector", lambda e, gn=gn, i=i: e.scalar_tensor_tensor(
                        zg_t, xg[i], sb["C_sb"][:, gn:gn + 1], zg_t, ALU.mult, ALU.add), inc="sV")
                scan_done[i] = vz
                last_va2 = va2
            ctx.op("vector", lambda e: e.tensor_mul(zg_t, c_t, zg_t))
            vy = ctx.op("vector", lambda e, g=g: e.scalar_tensor_tensor(
                yo_t, u_t, sb["D_sb"][:, g:g + 1], zg_t, ALU.mult, ALU.add), inc="sV")
            ctx.wait("sync", "sV", vy)
            ctx.wdma("sync", st_tok)
            st_tok = ctx.dma("sync", ys[row:row + 128, tc0:tc0 + TQ], yo_t)
            ctx.wdma("vector", st_tok)
    ctx.barrier()



def p_router1(ctx):
    sb, ps = ctx.sb, ctx.ps
    xn1f = ctx.dram["XN1F"].ap()
    xs = sb["slabB"]
    lg = sb["slabC"][0:2, 0:T]
    rt = sb["rt"]
    tgw = ctx.dma("sync", sb["tf3"][:, 0:32].rearrange("p (ch e) -> p ch e", e=2),
                  ctx.dram["ROUTERT"].ap().rearrange("(ch p) e -> p ch e", p=128))
    tid = ctx.dma("sync", sb["id_f"][:, :], ctx.dram["IDN"].ap()[:, :])
    ctx.wdma("tensor", [tgw, tid])
    fT = None
    for tq in range(NTQ):
        ctx.wait("sync", "sT", fT)
        toks = [ctx.dma("sync", sl(xs, c, TQ), xn1f[c * 128:(c + 1) * 128, tq * TQ:tq * TQ + TQ])
                for c in range(NC)]
        ctx.wdma("tensor", toks)
        pn = "A" if tq % 2 == 0 else "B"
        ctx.use_ps(pn)
        for c in range(NC):
            ctx.op("tensor", lambda e, c=c, pn=pn: e.matmul(
                ps[pn][0:2, :], sb["tf3"][:, c * 2:(c + 1) * 2], sl(xs, c, TQ),
                start=(c == 0), stop=(c == NC - 1)), inc="sT" if c == NC - 1 else None)
        fT = ctx.flush("tensor")
        ctx.wait("vector", "sT", fT)
        vlg = ctx.op("vector", lambda e, tq=tq, pn=pn: e.tensor_copy(lg[:, tq * TQ:tq * TQ + TQ], ps[pn][0:2, :]), inc="sV")
        ctx.ps_consumed(pn, "sV", vlg)
    vlg = ctx.flush("vector")
    ctx.wait("tensor", "sV", vlg)
    for tt in range(16):
        c0 = tt * 128
        ctx.use_ps("C")
        ctx.op("tensor", lambda e, c0=c0: e.matmul(
            ps["C"][:, 0:2], lg[0:2, c0:c0 + 128], sb["id_f"][0:2, 0:2], start=True, stop=True))
        fT = ctx.flush("tensor")
        ctx.wait("vector", "sT", fT)

        def V(fn):
            v = ctx.op("vector", fn, inc="sV")
            ctx.wait("vector", "sV", v)
            return v

        v0 = V(lambda e: e.tensor_copy(rt[:, 0:2], ps["C"][:, 0:2]))
        ctx.ps_consumed("C", "sV", v0)
        V(lambda e: e.reduce_max(rt[:, 32:33], rt[:, 0:2], axis=AX.X))
        v2 = V(lambda e: e.tensor_scalar(rt[:, 33:34], rt[:, 32:33], -1.0, None, ALU.mult))
        ctx.wait("scalar", "sV", v2)
        v3 = ctx.op("scalar", lambda e: e.activation(rt[:, 8:10], rt[:, 0:2], AF.Exp, bias=rt[:, 33:34]), inc="sA")
        fA = ctx.flush("scalar")
        ctx.wait("vector", "sA", fA)
        V(lambda e: e.reduce_sum(rt[:, 34:35], rt[:, 8:10], axis=AX.X))
        V(lambda e: e.reciprocal(rt[:, 35:36], rt[:, 34:35]))
        V(lambda e: e.tensor_scalar(rt[:, 8:10], rt[:, 8:10], rt[:, 35:36], None, ALU.mult))
        V(lambda e: e.reduce_max(rt[:, 36:37], rt[:, 8:10], axis=AX.X))
        V(lambda e: e.tensor_scalar(rt[:, 16:18], rt[:, 8:10], rt[:, 36:37], None, ALU.is_ge))
        vcm = V(lambda e: e.tensor_mul(rt[:, 16:18], rt[:, 16:18], rt[:, 8:10]))
        ctx.wait("tensor", "sV", vcm)
        ctx.use_ps("D")
        ctx.op("tensor", lambda e: e.matmul(ps["D"][0:2, 0:128], rt[:, 16:18], sb["id_f"][:, 0:128],
                                            start=True, stop=True))
        fT = ctx.flush("tensor")
        ctx.wait("vector", "sT", fT)
        vc2 = ctx.op("vector", lambda e, c0=c0: e.tensor_copy(sb["cmb_sb"][0:2, c0:c0 + 128], ps["D"][0:2, 0:128]), inc="sV")
        ctx.ps_consumed("D", "sV", vc2)
    vc2 = ctx.flush("vector")
    ctx.wait("sync", "sV", vc2)
    ctx.dma("sync", ctx.dram["CMB1"].ap()[0:2, :], sb["cmb_sb"][0:2, :])
    ctx.barrier()



def p_moe1a(ctx):
    sb, ps = ctx.sb, ctx.ps
    apan = ctx.dram["APAN"].ap()
    ldxn = load_xn(ctx, "XN1B")
    ctx.wdma("tensor", ldxn)
    tcm = ctx.dma("sync", sb["cmb_sb"][0:2, :], ctx.dram["CMB1"].ap()[0:2, :])
    ctx.wdma("tensor", tcm)
    cb_ = sb["slabA"]
    for e_ in range(2):
        for tq in range(NTQ):
            ctx.use_ps("C")
            ctx.op("tensor", lambda e, e_=e_, tq=tq: e.matmul(
                ps["C"][:, :], sb["sel8_sb"][0:2, e_ * 128:e_ * 128 + 128],
                sb["cmb_sb"][0:2, tq * TQ:tq * TQ + TQ], start=True, stop=True))
            fT = ctx.flush("tensor")
            ctx.wait("vector", "sT", fT)
            vcb = ctx.op("vector", lambda e, e_=e_, tq=tq: e.tensor_copy(sl(cb_, e_ * 4 + tq, TQ), ps["C"][:, :]), inc="sV")
            ctx.ps_consumed("C", "sV", vcb)
    vcb = ctx.flush("vector")
    ctx.wait("tensor", "sV", vcb)
    wb = [sl(sb["wbuf"], 0, 2048), sl(sb["wbuf"], 1, 2048)]
    wb_free = [None, None]
    ao = [sb["tb0"], sb["tb1"]]
    ao_free = [None, None]
    tf_free = None
    for e_ in range(2):
        for hk in range(64):
            bw = hk % 2
            ctx.wait("gpsimd", "sT", wb_free[bw])
            ldw = wcol_load(ctx, wb[bw], "WFCT", hk * 128, 128, row0=e_ * 16)
            ctx.wdma("tensor", ldw)
            for tq in range(NTQ):
                pn = "A" if tq % 2 == 0 else "B"
                ctx.use_ps(pn)
                for c in range(NC):
                    ctx.op("tensor", lambda e, c=c, bw=bw, tq=tq, pn=pn: e.matmul(
                        ps[pn][:, :], sl(wb[bw], c, 128), xn_tile(ctx, c, tq * TQ, TQ),
                        start=(c == 0), stop=(c == NC - 1)), inc="sT" if c == NC - 1 else None)
                vT = ctx.s["sT"].n
                if tq == NTQ - 1:
                    wb_free[bw] = vT
                i = tq % 2
                ctx.wait("scalar", "sT", vT)
                ctx.wait("scalar", "sV", tf_free)
                va1 = ctx.op("scalar", lambda e, pn=pn: e.activation(
                    sb["tf1"][:, :], ps[pn][:, :], AF.Abs, scale=ctx.cv(0.25)), inc="sA")
                ctx.wait("vector", "sA", va1)
                ctx.op("vector", lambda e, pn=pn: e.scalar_tensor_tensor(
                    sb["tf2"][:, :], ps[pn][:, :], 0.75, sb["tf1"][:, :], ALU.mult, ALU.add))
                fV = ctx.flush("vector")
                ctx.ps_consumed(pn, "sV", fV)
                ctx.wait("scalar", "sV", fV)
                va2 = ctx.op("scalar", lambda e: e.activation(sb["tf1"][:, :], sb["tf2"][:, :], AF.Square), inc="sA")
                ctx.wait("vector", "sA", va2)
                ctx.wdma("vector", ao_free[i])
                vap = ctx.op("vector", lambda e, e_=e_, tq=tq, i=i: e.tensor_tensor(
                    ao[i][:, :], sb["tf1"][:, :], sl(cb_, e_ * 4 + tq, TQ), ALU.mult), inc="sV")
                tf_free = vap
                ctx.wait("sync", "sV", vap)
                r0 = e_ * 8192 + hk * 128
                ao_free[i] = ctx.dma("sync", apan[r0:r0 + 128, tq * TQ:tq * TQ + TQ], ao[i][:, :])
    ctx.barrier()



def p_moe1b(ctx):
    sb, ps = ctx.sb, ctx.ps
    apan = ctx.dram["APAN"].ap()
    ysd = ctx.dram["YS"].ap()
    x1 = ctx.dram["X1"].ap()
    yout = ctx.dram["Y"].ap()
    pblk = sb["slabA"].bitcast(BF16)
    adb = sb["slabB"].bitcast(BF16)
    banks = ["A", "B", "C", "D"]
    ab_free = [None, None, None, None]
    ev_free = None
    for cop in range(8):
        ptoks = []
        for ci in range(2):
            co = cop * 2 + ci
            for e_ in range(2):
                dstb = pblk[:, (ci * 2 + e_) * 8192:(ci * 2 + e_ + 1) * 8192]
                ptoks.append(wcol_load(ctx, dstb, "WPROJT", co * 128, 128, nch=64, row0=e_ * 64))
        ctx.wdma("tensor", ptoks)
        for tqp in range(2):
            for bi, bn in enumerate(banks):
                ctx.use_ps(bn)
            first = True
            for e_ in range(2):
                for hk in range(64):
                    slot = (e_ * 64 + hk) % 4
                    ctx.wait("sync", "sT", ab_free[slot])
                    r0 = e_ * 8192 + hk * 128
                    atok = ctx.dma("sync", sl(adb, slot, 1024),
                                   apan[r0:r0 + 128, tqp * 1024:(tqp + 1) * 1024])
                    ctx.wdma("tensor", atok)
                    last = (e_ == 1 and hk == 63)
                    for ci in range(2):
                        for ti in range(2):
                            bn = banks[ci * 2 + ti]
                            inc = "sT" if (ci == 1 and ti == 1) else None
                            ctx.op("tensor", lambda e, ci=ci, ti=ti, e_=e_, hk=hk, slot=slot, bn=bn, first=first: e.matmul(
                                ps[bn][:, :], pblk[:, (ci * 2 + e_) * 8192 + hk * 128:(ci * 2 + e_) * 8192 + hk * 128 + 128],
                                sl(adb, slot, 1024)[:, ti * TQ:ti * TQ + TQ],
                                start=first, stop=(e_ == 1 and hk == 63)), inc=inc)
                    ab_free[slot] = ctx.s["sT"].n
                    first = False
            vT = ctx.s["sT"].n
            for ci in range(2):
                for ti in range(2):
                    bn = banks[ci * 2 + ti]
                    co = cop * 2 + ci
                    tcol = tqp * 1024 + ti * TQ
                    ctx.wait("scalar", "sT", vT)
                    ctx.wait("scalar", "sV", ev_free)
                    vg = ctx.op("scalar", lambda e, bn=bn: e.activation(sb["tf1"][:, :], ps[bn][:, :], AF.Silu), inc="sA")
                    ctx.ps_consumed(bn, "sA", vg)
                    tx = ctx.dma("sync", sb["tf2"][:, :], x1[co * 128:(co + 1) * 128, tcol:tcol + TQ])
                    ts_ = ctx.dma("sync", sb["tf3"][:, :], ysd[co * 128:(co + 1) * 128, tcol:tcol + TQ])
                    ctx.wdma("vector", [tx, ts_])
                    ctx.wait("vector", "sA", vg)
                    ctx.op("vector", lambda e: e.tensor_mul(sb["tf0"][:, :], sb["tf3"][:, :], sb["tf1"][:, :]))
                    vo = ctx.op("vector", lambda e: e.tensor_add(sb["tf1"][:, :], sb["tf2"][:, :], sb["tf0"][:, :]), inc="sV")
                    ev_free = vo
                    ctx.wait("sync", "sV", vo)
                    tout = ctx.dma("sync", yout[co * 128:(co + 1) * 128, tcol:tcol + TQ], sb["tf1"][:, :])
                    ctx.wdma("scalar", tout)
                    ctx.wdma("vector", tout)
        ctx.wait("gpsimd", "sT", ctx.s["sT"].n)
        ctx.wait("sync", "sT", ctx.s["sT"].n)
    ctx.barrier()



PHASES_ALL = ["ln0", "qkv", "att", "gateout", "router0", "ffn0",
              "ln1", "dtbc", "scan", "router1", "moe1a", "moe1b"]

WEIGHT_SPECS = {
    "WQKVT": (D, 6144), "WGATET": (D, D), "WOUTT": (D, D), "NORMW": (128, 16),
    "GATEWT": (D, 9),
    "FC1T": (8 * D, 1408), "FC2T": (8 * D, 1408), "PROJT": (8 * 1408, D),
    "SFC1T": (D, FK0 * 4 * 128), "SFC2T": (D, FK0 * 4 * 128), "SPROJT": (4 * 1408, D),
    "WDT": (D, D), "WB": (D, D), "WC": (D, D),
    "BDT": (128, 16), "BB": (128, 16), "BC": (128, 16),
    "ARAW": (D, 16), "BBASE": (D, 16), "CBASE": (D, 16), "DVEC": (128, 16),
    "ROUTERT": (D, 2), "WFCT": (2 * D, 8192), "WPROJT": (2 * 8192, D),
    "SLOPE": (1, 16), "IDXT": (BLK, BLK), "ARPOS": (128, 2), "ARNEG": (128, 2),
    "IDN": (128, 128), "SEL9": (9, 9 * 128),
}

SCRATCH_SPECS = {
    "XN0B": ((D, T), BF16), "STATS0": ((2, T), F32),
    "QT": ((D, T), BF16), "KT": ((D, T), BF16),
    "KTOK": ((NH * T, DH), BF16), "VTOK": ((NH * T, DH), BF16),
    "ATT": ((D, T), BF16), "SFIN": ((D, T), F32),
    "CMB": ((9, T), F32), "LGD": ((9, T), F32), "YMO": ((D, T), F32),
    "ZP": ((D, T), BF16), "RSDB": ((128, T), F32),
    "X1": ((D, T), F32),
    "XN1B": ((D, T), BF16), "XN1F": ((D, T), F32),
    "DTP": ((D, T), F32), "BTP": ((D, T), F32), "CTP": ((D, T), F32),
    "YS": ((D, T), F32),
    "CMB1": ((2, T), F32),
    "APAN": ((2 * 8192, T), BF16),
}


def build(phases, dbg=()):
    nc = bass.Bass("TRN2", target_bir_lowering=False)
    dram = {}
    dram["XIN"] = nc.dram_tensor("XIN", [D, T], F32, kind="ExternalInput")
    for name, shape in WEIGHT_SPECS.items():
        dram[name] = nc.dram_tensor(name, list(shape), F32, kind="ExternalInput")
    for name, (shape, dt) in SCRATCH_SPECS.items():
        kind = "ExternalOutput" if name in dbg else "Internal"
        dram[name] = nc.dram_tensor(name, list(shape), dt, kind=kind)
    dram["Y"] = nc.dram_tensor("Y", [D, T], F32, kind="ExternalOutput")

    from contextlib import ExitStack
    with ExitStack() as st:
        sb = {}

        def mk(name, shape, dt):
            sb[name] = st.enter_context(nc.sbuf_tensor(name, shape, dt))

        mk("slabA", [128, 16384], F32)
        mk("slabB", [128, 8192], F32)
        mk("slabC", [128, 8192], F32)
        mk("wbuf", [128, 4096], BF16)
        mk("wbuf2", [128, 4096], BF16)
        for t in ("tf0", "tf1", "tf2", "tf3"):
            mk(t, [128, TQ], F32)
        for t in ("tb0", "tb1", "tb2", "tb3"):
            mk(t, [128, TQ], BF16)
        for t in ("m_row", "v_row", "m2_row", "r_row"):
            mk(t, [1, TQ], F32)
        mk("ones128", [128, 1], F32)
        mk("cvals", [128, 12], F32)
        mk("junk", [1, 32], F32)
        mk("ones1", [1, 128], F32)
        mk("id_bf", [128, 128], BF16)
        mk("id_f", [128, 128], F32)
        mk("rt", [128, 64], F32)
        mk("qt_t", [128, BLK], BF16)
        mk("kt_t", [128, BLK], BF16)
        mk("ktok_t", [128, BLK], BF16)
        mk("vtok_t", [128, BLK], BF16)
        mk("qkm", [128, 2 * BLK], BF16)
        mk("kts", [128, BLK], BF16)
        mk("kv_f", [128, 128], F32)
        mk("kv_bf", [128, 128], BF16)
        mk("dd", [128, 2 * BLK], F32)
        mk("idx_sb", [128, 2 * BLK], F32)
        mk("ar_sb", [128, 2], F32)
        mk("arn_sb", [128, 2], F32)
        mk("negs", [128, 16], F32)
        mk("qdec", [128, 2], F32)
        mk("kdec", [128, 2], F32)
        mk("bdec", [128, 1], F32)
        mk("attbf", [128, BLK], BF16)
        mk("attfm", [128, BLK], BF16)
        mk("rstdb", [128, 4 * TQ], F32)
        mk("normw", [128, 16], F32)
        mk("gtile", [128, TQ], F32)
        mk("sel8_sb", [9, 9 * 128], F32)
        mk("cmb_sb", [9, T], F32)
        mk("cb", [128, T2], F32)
        mk("A_sb", [128, 256], F32)
        mk("B_sb", [128, 256], F32)
        mk("C_sb", [128, 256], F32)
        mk("D_sb", [128, 16], F32)
        mk("state", [128, 256], F32)
        mk("bia_sb", [128, 3 * 16], F32)

        sems = {}
        for s in ("bar", "sT", "sV", "sA", "sP"):
            sems[s] = SemC(st.enter_context(nc.semaphore(s)))
        sems["ringS"] = [st.enter_context(nc.semaphore(f"rs{i}")) for i in range(NSLOT)]
        sems["ringG"] = [st.enter_context(nc.semaphore(f"rg{i}")) for i in range(NSLOT)]

        ps = {}
        for name in ("A", "B", "C", "D"):
            ps[name] = st.enter_context(nc.psum_tensor("ps" + name, [128, 512], F32))
        ps["J"] = st.enter_context(nc.psum_tensor("psJ", [1, 16], F32))
        ps["T1"] = st.enter_context(nc.psum_tensor("psT1", [128, 1024], BF16))
        ps["T2"] = st.enter_context(nc.psum_tensor("psT2", [128, 1024], BF16))

        ctx = Ctx(nc, sems, dram, sb, ps)

        init_bar(ctx)
        for ph in phases:
            PHASE_FNS[ph](ctx)
        finish(ctx)

        with nc.Block() as block:
            for eng in ENGS:
                def body(e, eng=eng):
                    for fn in ctx.q[eng]:
                        fn(e)
                getattr(block, eng)(body)
    return nc


def init_bar(ctx):
    sb = ctx.sb
    ctx.op("vector", lambda e: e.memset(sb["ones128"][:, :], 1.0))
    ctx.op("vector", lambda e: e.memset(sb["ones1"][0:1, :], 1.0))
    for val, col in CVALS.items():
        ctx.op("vector", lambda e, val=val, col=col: e.memset(sb["cvals"][:, col:col + 1], float(val)))
    v = ctx.flush("vector")
    tok = ctx.dma("gpsimd", sb["id_bf"][:, :], ctx.dram["IDN"].ap()[:, :])
    tok2 = ctx.dma("sync", sb["sel8_sb"][0:9, :], ctx.dram["SEL9"].ap()[:, :])
    for eng in ENGS:
        ctx.wait(eng, "sV", v)
        ctx.wdma(eng, [tok, tok2])


def finish(ctx):
    ctx.wdma("sync", [("S", i, s.n) for i, s in enumerate(ctx.dring["S"]) if s.n])
    ctx.wdma("sync", [("G", i, s.n) for i, s in enumerate(ctx.dring["G"]) if s.n])


def _phase_ln0(ctx):
    p_layernorm(ctx, "XIN", "XN0B", stats_name="STATS0")


def _phase_qkv(ctx):
    p_qkv(ctx)


def _phase_att(ctx):
    p_attention(ctx)


def _phase_gateout(ctx):
    p_gate_out(ctx)


def _phase_att_chain(ctx):
    p_qkv(ctx)
    p_attention(ctx)
    p_gate_out(ctx)


PHASE_FNS = {
    "ln0": _phase_ln0,
    "qkv": _phase_qkv,
    "att": _phase_att,
    "gateout": _phase_gateout,
    "att_chain": _phase_att_chain,
    "router0": p_router0,
    "ffn0": p_moe0,
    "ln1": lambda ctx: p_layernorm(ctx, "X1", "XN1B", dst_f32="XN1F"),
    "dtbc": p_dtbc,
    "scan": p_scan,
    "router1": p_router1,
    "moe1a": p_moe1a,
    "moe1b": p_moe1b,
}



def _tp(a):
    return np.ascontiguousarray(np.asarray(a).T.astype(np.float32))


def prep_shared(inp):
    H = {}
    H["WQKVT"] = _tp(inp["L0_attn_w_qkv"])
    H["WGATET"] = _tp(inp["L0_attn_w_gate"])
    H["WOUTT"] = _tp(inp["L0_attn_w_out"])
    H["NORMW"] = np.ascontiguousarray(np.asarray(inp["L0_attn_norm_w"], dtype=np.float32).reshape(16, 128).T)
    gw9 = np.concatenate([np.asarray(inp["L0_ffn_gate_w"], dtype=np.float32),
                          np.asarray(inp["L0_ffn_shared_gate_w"], dtype=np.float32)], axis=0)
    H["GATEWT"] = np.ascontiguousarray(gw9.T)
    fc1 = np.asarray(inp["L0_ffn_expert_fc1"], dtype=np.float32)
    fc2 = np.asarray(inp["L0_ffn_expert_fc2"], dtype=np.float32)
    prj = np.asarray(inp["L0_ffn_expert_proj"], dtype=np.float32)
    H["FC1T"] = np.ascontiguousarray(fc1.transpose(0, 2, 1)).reshape(8 * D, 1408)
    H["FC2T"] = np.ascontiguousarray(fc2.transpose(0, 2, 1)).reshape(8 * D, 1408)
    H["PROJT"] = np.ascontiguousarray(prj.transpose(0, 2, 1)).reshape(8 * 1408, D)
    H["SFC1T"] = _tp(inp["L0_ffn_shared_fc1"])
    H["SFC2T"] = _tp(inp["L0_ffn_shared_fc2"])
    H["SPROJT"] = _tp(inp["L0_ffn_shared_proj"])
    H["WDT"] = np.ascontiguousarray(np.asarray(inp["L1_attn_Wdt"], dtype=np.float32))
    H["WB"] = np.ascontiguousarray(np.asarray(inp["L1_attn_Wb"], dtype=np.float32))
    H["WC"] = np.ascontiguousarray(np.asarray(inp["L1_attn_Wc"], dtype=np.float32))
    H["BDT"] = np.ascontiguousarray(np.asarray(inp["L1_attn_bdt"], dtype=np.float32).reshape(16, 128).T)
    H["BB"] = np.ascontiguousarray(np.asarray(inp["L1_attn_bb"], dtype=np.float32).reshape(16, 128).T)
    H["BC"] = np.ascontiguousarray(np.asarray(inp["L1_attn_bc"], dtype=np.float32).reshape(16, 128).T)
    H["ARAW"] = np.ascontiguousarray(np.asarray(inp["L1_attn_a_raw"], dtype=np.float32))
    H["BBASE"] = np.ascontiguousarray(np.asarray(inp["L1_attn_B_base"], dtype=np.float32))
    H["CBASE"] = np.ascontiguousarray(np.asarray(inp["L1_attn_C_base"], dtype=np.float32))
    H["DVEC"] = np.ascontiguousarray(np.asarray(inp["L1_attn_D"], dtype=np.float32).reshape(16, 128).T)
    H["ROUTERT"] = _tp(inp["L1_ffn_w_router"])
    wfc = np.asarray(inp["L1_ffn_w_fc"], dtype=np.float32)
    wpr = np.asarray(inp["L1_ffn_w_proj"], dtype=np.float32)
    H["WFCT"] = np.ascontiguousarray(wfc.transpose(0, 2, 1)).reshape(2 * D, 8192)
    H["WPROJT"] = np.ascontiguousarray(wpr.transpose(0, 2, 1)).reshape(2 * 8192, D)
    H["SLOPE"] = np.asarray(inp["L0_attn_slope_rate"], dtype=np.float32).reshape(1, 16)
    m = np.arange(1, BLK + 1, dtype=np.float32)
    idx = m[:, None] - m[None, :]
    idxT = idx.T.copy()
    idxT[idxT < 0] = 1e9
    H["IDXT"] = np.ascontiguousarray(idxT.astype(np.float32))
    H["ARPOS"] = np.ascontiguousarray(m.reshape(2, 128).T)
    H["ARNEG"] = np.ascontiguousarray((BLK - m).reshape(2, 128).T)
    H["IDN"] = np.eye(128, dtype=np.float32)
    sel = np.zeros((9, 9 * 128), dtype=np.float32)
    for e in range(9):
        sel[e, e * 128:(e + 1) * 128] = 1.0
    H["SEL9"] = sel
    return H


def run(inputs, phases, dbg=()):
    H = prep_shared(inputs)
    x = np.asarray(inputs["L0_attn_hidden_states"], dtype=np.float32)
    nc = build(phases, dbg=dbg)
    in_maps = []
    for core in range(8):
        m = {"XIN": np.ascontiguousarray(x[core].T)}
        m.update(H)
        in_maps.append(m)
    res = run_bass_kernel_spmd(nc, in_maps, core_ids=list(range(8)))
    return res.results


def kernel(**inputs):
    results = run(inputs, PHASES_ALL)
    out = np.stack([np.ascontiguousarray(np.asarray(r["Y"]).T) for r in results], axis=0)
    return out.astype(np.float32)
